# version 21
# baseline (speedup 1.0000x reference)
import numpy as np
import ml_dtypes
import concourse.bass as bass
import concourse.mybir as mybir
from concourse.bass_utils import run_bass_kernel_spmd

F32 = mybir.dt.float32
BF16 = mybir.dt.bfloat16
I32 = mybir.dt.int32
AF = mybir.ActivationFunctionType
ALU = mybir.AluOpType
AX = mybir.AxisListType

D = 1024
NB = 8192
HID = 3072
SQRT_N = float(np.sqrt(8192.0))
LN_SQRT_N = float(np.log(8192.0) / 2.0)
FFN_SCALE = 1.0 / 127.0
EPS = 1e-5
MAGIC = 0x5F3759DF


def _newton_rsqrt(vq, vs, dst, src, t1):
    vs(vq.tensor_scalar(out=t1.bitcast(I32), in0=src.bitcast(I32),
                        scalar1=1, scalar2=None, op0=ALU.arith_shift_right))
    vs(vq.tensor_scalar(out=dst.bitcast(I32), in0=t1.bitcast(I32),
                        scalar1=-1, scalar2=MAGIC, op0=ALU.mult, op1=ALU.add))
    for _ in range(2):
        vs(vq.tensor_tensor(out=t1, in0=dst, in1=dst, op=ALU.mult))
        vs(vq.tensor_tensor(out=t1, in0=t1, in1=src, op=ALU.mult))
        vs(vq.tensor_scalar(out=t1, in0=t1, scalar1=-0.5, scalar2=1.5,
                            op0=ALU.mult, op1=ALU.add))
        vs(vq.tensor_tensor(out=dst, in0=dst, in1=t1, op=ALU.mult))


def build(nt=64, dbg=False):
    T = nt * 128
    ng = nt // 4
    nh = nt // 2
    nc = bass.Bass("TRN2", target_bir_lowering=False)

    x_h = nc.dram_tensor("x", [T, D], F32, kind="ExternalInput").ap()
    kT_h = nc.dram_tensor("kT", [D, NB], F32, kind="ExternalInput").ap()
    v_h = nc.dram_tensor("v", [NB, D], F32, kind="ExternalInput").ap()
    wgT_h = nc.dram_tensor("wgT", [D, HID], I32, kind="ExternalInput").ap()
    wuT_h = nc.dram_tensor("wuT", [D, HID], I32, kind="ExternalInput").ap()
    wdT_h = nc.dram_tensor("wdT", [HID, D], I32, kind="ExternalInput").ap()
    id_h = nc.dram_tensor("ident", [128, 128], BF16, kind="ExternalInput").ap()
    out_h = nc.dram_tensor("out", [T, D], F32, kind="ExternalOutput").ap()

    scrk = "ExternalOutput" if dbg else "Internal"
    if dbg:
        dbg_xn = nc.dram_tensor("dbg_xn", [128, 2048], BF16, kind="ExternalOutput").ap()
        dbg_x = nc.dram_tensor("dbg_x", [128, 2048], F32, kind="ExternalOutput").ap()
        dbg_cols = nc.dram_tensor("dbg_cols", [128, 16], F32, kind="ExternalOutput").ap()
        dbg_ssq = nc.dram_tensor("dbg_ssq", [128, 8], F32, kind="ExternalOutput").ap()
        dbg_inv = nc.dram_tensor("dbg_inv", [128, 2], F32, kind="ExternalOutput").ap()
    g_h = nc.dram_tensor("g_scr", [T, NB], BF16, kind=scrk).ap()
    xnT_h = nc.dram_tensor("xnT_scr", [nt, 128, D], BF16, kind=scrk).ap()
    s_h = nc.dram_tensor("s_scr", [T, D], BF16, kind=scrk).ap()

    import contextlib
    ctx = contextlib.ExitStack()
    with ctx:
        arena = ctx.enter_context(nc.sbuf_tensor("arena", [128, 73728], BF16))
        work = ctx.enter_context(nc.sbuf_tensor("work", [128, 30720], BF16))
        ident = ctx.enter_context(nc.sbuf_tensor("ident_sb", [128, 128], BF16))
        
        cols = ctx.enter_context(nc.sbuf_tensor("cols", [128, 16], F32))
        ssq = ctx.enter_context(nc.sbuf_tensor("ssq", [128, 8], F32))
        inv2 = ctx.enter_context(nc.sbuf_tensor("inv2", [128, 2], F32))
        sg = ctx.enter_context(nc.sbuf_tensor("sg", [128, 512], BF16))
        su = ctx.enter_context(nc.sbuf_tensor("su", [128, 512], BF16))
        P = ctx.enter_context(nc.psum_tensor("ps", [128, 4096], F32))

        jk = work[:, 24576:25600]
        sqb = work[:, 28672:30720]
        x2f = work[:, 0:4096].bitcast(F32)
        xn2 = work[:, 4096:6144]
        xnT2 = work[:, 6144:8192]
        sc = work[:, 8192:16384]
        gt = work[:, 16384:24576]
        ring = work[:, 25600:28672]
        sB = work[:, 16384:24576]
        cxnT = work[:, 0:4096]
        cs = work[:, 4096:8192]
        cxf = work[:, 8192:16384].bitcast(F32)
        ch = work[:, 16384:22528]
        cgf = work[:, 22528:26624].bitcast(F32)
        prodf = work[:, 26624:30720].bitcast(F32)

        c_sum = cols[:, 0:1]
        c_mean = cols[:, 1:2]
        sqs2 = cols[:, 2:4]
        c_var = cols[:, 4:5]
        c_rstd = cols[:, 5:6]
        c_t1 = cols[:, 6:7]
        c_ss = cols[:, 8:9]
        c_y = cols[:, 9:10]
        c_jk = cols[:, 10:11]

        p_sc = [P[:, 0:1024], P[:, 1024:2048], P[:, 2048:3072]]
        p_tr = [P[:, 3072:3584].bitcast(BF16), P[:, 3584:4096].bitcast(BF16)]
        s_ps = [P[:, tb * 1024:(tb + 1) * 1024] for tb in range(4)]
        pg_ = [P[:, 0:256], P[:, 512:768]]
        pu_ = [P[:, 1024:1280], P[:, 1536:1792]]
        pd_ = [P[:, 2048 + i * 512:2048 + (i + 1) * 512] for i in range(4)]

        sems = {}
        for name in ["semX", "semW", "semSPX", "semSPG", "semSPS",
                     "semT0", "semT1", "semT2", "semT3", "semT4", "semT5",
                     "semCL", "semOUT", "semLN", "semLNS", "semTR", "semXNT",
                     "semSC", "semEV", "semINV", "semGEL", "semBMM", "semSEV",
                     "semCP1", "semCEV", "semCH", "semCP2", "semCGE", "semCFIN",
                     "semV", "semSQ", "semGL", "semSPH", "semSS", "semSR"]:
            sems[name] = ctx.enter_context(nc.semaphore(name))
        S = type("S", (), sems)
        semT = [sems["semT%d" % s] for s in range(6)]

        blk = ctx.enter_context(nc.Block())

        @blk.sync
        def _(q):
            q.dma_start(out=ident[:], in_=id_h).then_inc(S.semX, 16)
            q.wait_ge(S.semX, 16)
            for j in range(min(2, nt)):
                q.dma_start(out=x2f[:, (j % 2) * 1024:(j % 2 + 1) * 1024],
                            in_=x_h[j * 128:(j + 1) * 128, :]).then_inc(S.semX, 16)
                q.wait_ge(S.semX, 16 * (j + 2))
            for i in range(nt):
                if 8 <= i < 14:
                    k = i - 8
                    if k == 0:
                        q.wait_ge(S.semSPG, 128)
                    q.dma_start_transpose(
                        ring[:, k * 512:(k + 1) * 512],
                        g_h[0:512, k * 128:(k + 1) * 128],
                    ).then_inc(semT[k], 16)
                q.wait_ge(S.semXNT, i + 1)
                q.dma_start(out=xnT_h[i], in_=xnT2[:, (i % 2) * 1024:(i % 2 + 1) * 1024]
                            ).then_inc(S.semSPX, 16)
                q.wait_ge(S.semSPX, 16 * (i + 1))
                j = i + 2
                if j < nt:
                    q.wait_ge(S.semLN, i + 1)
                    q.wait_ge(S.semLNS, i + 1)
                    q.dma_start(out=x2f[:, (j % 2) * 1024:(j % 2 + 1) * 1024],
                                in_=x_h[j * 128:(j + 1) * 128, :]).then_inc(S.semX, 16)
                    q.wait_ge(S.semX, 16 * (j + 2))
            def ld_cxnT(h):
                q.dma_start(
                    out=cxnT[:, (h % 2) * 2048:(h % 2 + 1) * 2048].rearrange(
                        "p (c T t) -> p c T t", c=8, T=2),
                    in_=xnT_h[2 * h:2 * h + 2].rearrange(
                        "T p (c t) -> p c T t", c=8),
                ).then_inc(S.semCL, 16)

            def ld_cs(h):
                q.dma_start(
                    out=cs[:, (h % 2) * 2048:(h % 2 + 1) * 2048].rearrange(
                        "p (tb o) -> p tb o", tb=2),
                    in_=s_h[256 * h:256 * (h + 1), :].rearrange(
                        "(tb p) o -> p tb o", p=128),
                ).then_inc(S.semCL, 16)

            def ld_cxf(h):
                q.dma_start(
                    out=cxf[:, (h % 2) * 2048:(h % 2 + 1) * 2048].rearrange(
                        "p (tb o) -> p tb o", tb=2),
                    in_=x_h[256 * h:256 * (h + 1), :].rearrange(
                        "(tb p) o -> p tb o", p=128),
                ).then_inc(S.semCL, 16)

            for g in range(ng):
                if g == 1:
                    q.wait_ge(S.semSPX, 16 * nt)
                    q.wait_ge(S.semLN, nt)
                    q.wait_ge(S.semLNS, nt)
                    ld_cxnT(0)
                if g == 2:
                    q.wait_ge(S.semTR, nt)
                    q.wait_ge(S.semSC, 8 * nt)
                    q.wait_ge(S.semSPS, 16)
                    q.wait_ge(S.semCL, 16)
                    ld_cs(0)
                if g == 3:
                    q.wait_ge(S.semGEL, nt)
                    q.wait_ge(S.semCL, 32)
                    ld_cxf(0)
                if g == 4:
                    q.wait_ge(S.semCL, 48)
                    ld_cxnT(1)
                if g == 5:
                    q.wait_ge(S.semCL, 64)
                    ld_cs(1)
                if g == 6:
                    q.wait_ge(S.semCL, 80)
                    ld_cxf(1)
                for c in range(64):
                    if g == 0 and c < 6:
                        continue
                    G = 64 * g + c
                    if c == 0:
                        q.wait_ge(S.semSPG, 32 * (4 * g + 4))
                    if c == 32:
                        q.wait_ge(S.semSPH, 32 * (4 * g + 4))
                    if G >= 6:
                        q.wait_ge(S.semBMM, G - 5)
                    q.dma_start_transpose(
                        ring[:, (G % 6) * 512:(G % 6 + 1) * 512],
                        g_h[512 * g:512 * (g + 1), c * 128:(c + 1) * 128],
                    ).then_inc(semT[G % 6], 16)
            for h in range(2, nh):
                q.wait_ge(S.semOUT, 32 * (h - 1))
                q.wait_ge(S.semSPS, 16 * ng)
                ld_cxnT(h)
                q.wait_ge(S.semCL, 48 * h + 16)
                ld_cs(h)
                q.wait_ge(S.semCL, 48 * h + 32)
                ld_cxf(h)
            q.wait_ge(S.semOUT, 32 * nh)
            if dbg:
                q.dma_start(out=dbg_xn, in_=xn2).then_inc(S.semX, 16)
                q.dma_start(out=dbg_x, in_=x2f).then_inc(S.semX, 16)
                q.dma_start(out=dbg_cols, in_=cols[:]).then_inc(S.semX, 16)
                q.dma_start(out=dbg_ssq, in_=ssq[:]).then_inc(S.semX, 16)
                q.dma_start(out=dbg_inv, in_=inv2[:]).then_inc(S.semX, 16)
                q.wait_ge(S.semX, 16 * (nt + 1) + 80)

        @blk.gpsimd
        def _(q):
            kT_arena = arena[:, 0:65536].rearrange("p (d n) -> p d n", d=8)
            for j in range(8):
                q.dma_start(out=kT_arena[:, :, j * 1024:(j + 1) * 1024],
                            in_=kT_h[:, j * 1024:(j + 1) * 1024].rearrange(
                                "(d p) n -> p d n", p=128)).then_inc(S.semW, 16)
                q.wait_ge(S.semW, 16 * (j + 1))
            for i in range(nt - 1):
                q.wait_ge(S.semGL, i + 1)
                q.dma_start(out=g_h[128 * i:128 * (i + 1), 0:2048],
                            in_=gt[:, 0:2048]).then_inc(S.semSPG, 16)
                q.dma_start(out=g_h[128 * i:128 * (i + 1), 2048:4096],
                            in_=gt[:, 2048:4096]).then_inc(S.semSPG, 16)
            q.wait_ge(S.semSC, 8 * nt)
            for m in range(16):
                q.dma_start(
                    out=arena[:, m * 4096:(m + 1) * 4096].rearrange(
                        "p (c o) -> p c o", c=4),
                    in_=v_h[512 * m:512 * (m + 1), :].rearrange(
                        "(c p) o -> p c o", p=128),
                ).then_inc(S.semW, 16)
                q.wait_ge(S.semW, 128 + 16 * (m + 1))
            q.wait_ge(S.semGL, nt)
            q.dma_start(out=g_h[128 * (nt - 1):128 * nt, 0:2048],
                        in_=gt[:, 0:2048]).then_inc(S.semSPG, 16)
            q.dma_start(out=g_h[128 * (nt - 1):128 * nt, 2048:4096],
                        in_=gt[:, 2048:4096]).then_inc(S.semSPG, 16)
            wg_arena = arena[:, 0:24576].rearrange("p (d hh) -> p d hh", d=8)
            wu_arena = arena[:, 24576:49152].rearrange("p (d hh) -> p d hh", d=8)
            q.wait_ge(S.semBMM, 64 * (ng - 1) + 24)
            for j in range(4):
                q.dma_start(
                    out=wg_arena[:, :, 768 * j:768 * (j + 1)],
                    in_=wgT_h[:, 768 * j:768 * (j + 1)].rearrange(
                        "(d p) hh -> p d hh", p=128),
                ).then_inc(S.semW, 16)
                q.wait_ge(S.semW, 384 + 16 * (j + 1))
            q.wait_ge(S.semBMM, 64 * (ng - 1) + 48)
            for j in range(4):
                q.dma_start(
                    out=wu_arena[:, :, 768 * j:768 * (j + 1)],
                    in_=wuT_h[:, 768 * j:768 * (j + 1)].rearrange(
                        "(d p) hh -> p d hh", p=128),
                ).then_inc(S.semW, 16)
                q.wait_ge(S.semW, 448 + 16 * (j + 1))
            q.wait_ge(S.semBMM, 64 * ng)
            for j in range(4):
                q.dma_start(
                    out=arena[:, 49152 + 6 * j * 1024:49152 + 6 * (j + 1) * 1024
                              ].rearrange("p (c o) -> p c o", c=6),
                    in_=wdT_h[768 * j:768 * (j + 1), :].rearrange(
                        "(c p) o -> p c o", p=128),
                ).then_inc(S.semW, 16)
                q.wait_ge(S.semW, 512 + 16 * (j + 1))
            for h in range(nh):
                for tb in range(2):
                    q.wait_ge(S.semCFIN, 2 * h + tb + 1)
                    q.dma_start(
                        out=out_h[256 * h + 128 * tb:256 * h + 128 * (tb + 1), :],
                        in_=cxf[:, (h % 2) * 2048 + tb * 1024:
                                (h % 2) * 2048 + (tb + 1) * 1024],
                    ).then_inc(S.semOUT, 16)
            q.wait_ge(S.semOUT, 32 * nh)

        @blk.tensor
        def _(q):
            def emit_tr(j):
                q.wait_ge(S.semLN, j + 1)
                if j >= 2:
                    q.wait_ge(S.semXNT, j - 1)
                inst = None
                for c in range(8):
                    inst = q.transpose(
                        p_tr[j % 2][:, c * 128:(c + 1) * 128],
                        xn2[:, (j % 2) * 1024 + c * 128:(j % 2) * 1024 + (c + 1) * 128],
                        ident[:])
                inst.then_inc(S.semTR, 1)

            for i in range(nt):
                if i == 0:
                    emit_tr(0)
                if i + 1 < nt:
                    emit_tr(i + 1)
                q.wait_ge(S.semXNT, i + 1)
                for qq in range(8):
                    g8 = 8 * i + qq
                    if i < 4:
                        q.wait_ge(S.semW, 16 * (qq + 1))
                    if qq >= 3:
                        q.wait_ge(S.semEV, 8 * i + qq - 2)
                    elif i >= 1:
                        q.wait_ge(S.semEV, 8 * (i - 1) + {0: 6, 1: 7, 2: 5}[qq] + 1)
                    ps = p_sc[qq % 3]
                    inst = None
                    for d in range(8):
                        stat = xnT2[:, (i % 2) * 1024 + d * 128:
                                    (i % 2) * 1024 + (d + 1) * 128]
                        for nb in range(2):
                            n0 = qq * 1024 + nb * 512
                            inst = q.matmul(
                                ps[:, nb * 512:(nb + 1) * 512], stat,
                                arena[:, d * 8192 + n0:d * 8192 + n0 + 512],
                                start=(d == 0), stop=(d == 7))
                    inst.then_inc(S.semSC, 1)
            for g in range(ng):
                for c in range(64):
                    G = 64 * g + c
                    q.wait_ge(semT[G % 6], 16 * (G // 6 + 1))
                    if c % 4 == 0:
                        m = c // 4
                        q.wait_ge(S.semW, 128 + 16 * (m + 1))
                    if c == 0:
                        if g == 0:
                            q.wait_ge(S.semEV, 8 * nt)
                        if g >= 1:
                            q.wait_ge(S.semSEV, 4 * g)
                    slot = ring[:, (G % 6) * 512:(G % 6 + 1) * 512]
                    inst = None
                    for tb in range(4):
                        stat = slot[:, tb * 128:(tb + 1) * 128]
                        for oh in range(2):
                            inst = q.matmul(
                                s_ps[tb][:, oh * 512:(oh + 1) * 512], stat,
                                arena[:, c * 1024 + oh * 512:c * 1024 + oh * 512 + 512],
                                start=(c == 0), stop=(c == 63))
                    inst.then_inc(S.semBMM, 1)
            for h in range(nh):
                for hb in range(24):
                    G2 = 24 * h + hb
                    if hb == 0:
                        q.wait_ge(S.semCL, 48 * h + 16)
                        if h == 0:
                            q.wait_ge(S.semSEV, 4 * ng)
                    if h == 0:
                        q.wait_ge(S.semW, 384 + 16 * (hb // 6 + 1))
                    if G2 >= 2:
                        q.wait_ge(S.semCEV, G2 - 1)
                        q.wait_ge(S.semCH, G2 - 1)
                    pg = pg_[hb % 2]
                    pu = pu_[hb % 2]
                    inst = None
                    for d in range(8):
                        mv = cxnT[:, (h % 2) * 2048 + d * 256:
                                  (h % 2) * 2048 + (d + 1) * 256]
                        q.matmul(pg, arena[:, d * 3072 + hb * 128:
                                           d * 3072 + (hb + 1) * 128],
                                 mv, start=(d == 0), stop=(d == 7))
                    if h == 0:
                        q.wait_ge(S.semW,
                                  448 + 16 * (hb // 6 + 1))
                    for d in range(8):
                        mv = cxnT[:, (h % 2) * 2048 + d * 256:
                                  (h % 2) * 2048 + (d + 1) * 256]
                        inst = q.matmul(pu, arena[:, 24576 + d * 3072 + hb * 128:
                                                  24576 + d * 3072 + (hb + 1) * 128],
                                        mv, start=(d == 0), stop=(d == 7))
                    inst.then_inc(S.semCP1, 1)
                for idx in range(4):
                    tb, oh = divmod(idx, 2)
                    if h >= 1:
                        q.wait_ge(S.semCGE, 4 * (h - 1) + idx + 1)
                    inst = None
                    for hb in range(24):
                        if idx == 0:
                            q.wait_ge(S.semCH, 24 * h + hb + 1)
                        if h == 0 and hb % 6 == 0:
                            q.wait_ge(S.semW, 512 + 16 * (hb // 6 + 1))
                        inst = q.matmul(
                            pd_[idx],
                            ch[:, hb * 256 + tb * 128:hb * 256 + (tb + 1) * 128],
                            arena[:, 49152 + hb * 1024 + oh * 512:
                                  49152 + hb * 1024 + oh * 512 + 512],
                            start=(hb == 0), stop=(hb == 23))
                    inst.then_inc(S.semCP2, 1)

        @blk.scalar
        def _(q):
            def emit_sq(j):
                q.wait_ge(S.semX, 16 * (j + 2))
                if j >= 2:
                    q.wait_ge(S.semLN, j - 1)
                q.activation(out=jk[:], in_=x2f[:, (j % 2) * 1024:(j % 2 + 1) * 1024],
                             func=AF.Square,
                             accum_out=sqs2[:, (j % 2):(j % 2) + 1]
                             ).then_inc(S.semLNS, 1)

            def emit_ge8(i, e):
                q.wait_ge(S.semEV, 8 * i + e + 1)
                if e == 0:
                    q.wait_ge(S.semINV, i + 1)
                    if i >= 1:
                        q.wait_ge(S.semSPG, 32 * i)
                if e == 4 and i >= 1:
                    q.wait_ge(S.semSPH, 32 * i)
                inst = q.activation(
                    out=gt[:, e * 1024:(e + 1) * 1024],
                    in_=sc[:, e * 1024:(e + 1) * 1024], func=AF.Gelu,
                    scale=inv2[:, (i % 2):(i % 2) + 1])
                if e == 3:
                    inst.then_inc(S.semGL, 1)
                if e == 7:
                    inst.then_inc(S.semGEL, 1)
                    q.wait_ge(S.semGEL, i + 1)
                    q.dma_start(out=g_h[128 * i:128 * (i + 1), 4096:6144],
                                in_=gt[:, 4096:6144]).then_inc(S.semSPH, 16)
                    q.dma_start(out=g_h[128 * i:128 * (i + 1), 6144:8192],
                                in_=gt[:, 6144:8192]).then_inc(S.semSPH, 16)

            def emit_ev(i, qq):
                q.wait_ge(S.semSC, 8 * i + qq + 1)
                q.activation(out=sc[:, qq * 1024:(qq + 1) * 1024],
                             in_=p_sc[qq % 3], func=AF.Copy
                             ).then_inc(S.semEV, 1)

            for j in range(min(2, nt)):
                emit_sq(j)
            for i in range(nt):
                for e in range(4):
                    if i >= 1:
                        emit_ge8(i - 1, e)
                    emit_ev(i, e)
                if i >= 1:
                    emit_ge8(i - 1, 4)
                if i + 2 < nt:
                    emit_sq(i + 2)
                if i >= 1:
                    emit_ge8(i - 1, 5)
                emit_ev(i, 4)
                if i >= 1:
                    emit_ge8(i - 1, 6)
                emit_ev(i, 5)
                if i >= 1:
                    emit_ge8(i - 1, 7)
                    q.activation(out=c_jk, in_=sqs2[:, 0:1], func=AF.Ln)
                emit_ev(i, 6)
                q.wait_ge(S.semEV, 8 * i + 7)
                q.activation(out=jk[:], in_=sc[:, 6144:7168], func=AF.Square,
                             accum_out=ssq[:, 3:4])
                emit_ev(i, 7)
                q.wait_ge(S.semEV, 8 * (i + 1))
                q.activation(out=jk[:], in_=sc[:, 7168:8192], func=AF.Square,
                             accum_out=ssq[:, 4:5]).then_inc(S.semSQ, 1)
                q.wait_ge(S.semSS, i + 1)
                q.activation(out=c_y, in_=c_ss, func=AF.Ln,
                             scale=1.0 / NB).then_inc(S.semSR, 1)
                q.wait_ge(S.semSR, i + 1)
                q.activation(out=inv2[:, (i % 2):(i % 2) + 1], in_=c_y,
                             func=AF.Exp, scale=-0.5).then_inc(S.semINV, 1)
            for e in range(8):
                emit_ge8(nt - 1, e)
            for g in range(ng):
                for tb in range(2):
                    q.wait_ge(S.semBMM, 64 * (g + 1))
                    if g == 0 and tb == 0:
                        q.wait_ge(S.semSPG, 32 * nt)
                    if g == 1 and tb == 0:
                        q.wait_ge(S.semSPH, 32 * nt)
                    if g >= 2:
                        q.wait_ge(S.semSPS, 16 * (g - 1))
                    q.activation(out=sB[:, (g % 2) * 4096 + tb * 1024:
                                       (g % 2) * 4096 + (tb + 1) * 1024],
                                 in_=s_ps[tb], func=AF.Copy).then_inc(S.semSEV, 1)
                q.wait_ge(S.semSEV, 4 * (g + 1))
                q.dma_start(
                    out=s_h[512 * g:512 * (g + 1), :].rearrange(
                        "(tb p) o -> p tb o", p=128),
                    in_=sB[:, (g % 2) * 4096:(g % 2 + 1) * 4096].rearrange(
                        "p (tb o) -> p tb o", tb=4),
                ).then_inc(S.semSPS, 16)
            for h in range(nh):
                for hb in range(24):
                    G2 = 24 * h + hb
                    q.wait_ge(S.semCP1, G2 + 1)
                    if G2 >= 2:
                        q.wait_ge(S.semCH, G2 - 1)
                    q.activation(out=sg[:, (hb % 2) * 256:(hb % 2 + 1) * 256],
                                 in_=pg_[hb % 2], func=AF.Silu, scale=FFN_SCALE
                                 ).then_inc(S.semCEV, 1)
                for idx in range(4):
                    tb, oh = divmod(idx, 2)
                    q.wait_ge(S.semCP2, 4 * h + idx + 1)
                    if idx == 0 and h == 0:
                        q.wait_ge(S.semSPS, 16 * ng)
                    if idx == 0 and h >= 1:
                        q.wait_ge(S.semCFIN, 2 * h)
                    q.activation(out=cgf[:, tb * 1024 + oh * 512:
                                        tb * 1024 + oh * 512 + 512],
                                 in_=pd_[idx], func=AF.Silu,
                                 scale=FFN_SCALE * FFN_SCALE
                                 ).then_inc(S.semCGE, 1)

        @blk.vector
        def _(q):
            vcnt = [0]

            def vs(inst):
                vcnt[0] += 1
                inst.then_inc(S.semV, 1)
                q.wait_ge(S.semV, vcnt[0])

            def emit_ln(j):
                xs = x2f[:, (j % 2) * 1024:(j % 2 + 1) * 1024]
                q.wait_ge(S.semX, 16 * (j + 2))
                vs(q.reduce_sum(out=c_sum, in_=xs, axis=AX.X))
                vs(q.tensor_scalar(out=c_mean, in0=c_sum, scalar1=1.0 / D,
                                   scalar2=None, op0=ALU.mult))
                q.wait_ge(S.semLNS, j + 1)
                vs(q.tensor_scalar(out=c_var, in0=sqs2[:, (j % 2):(j % 2) + 1],
                                   scalar1=1.0 / D, scalar2=None, op0=ALU.mult))
                vs(q.tensor_tensor(out=c_t1, in0=c_mean, in1=c_mean, op=ALU.mult))
                vs(q.tensor_tensor(out=c_var, in0=c_var, in1=c_t1, op=ALU.subtract))
                vs(q.tensor_scalar(out=c_var, in0=c_var, scalar1=EPS, scalar2=None,
                                   op0=ALU.add))
                _newton_rsqrt(q, vs, c_rstd, c_var, c_t1)
                if j >= 2:
                    q.wait_ge(S.semTR, j - 1)
                q.tensor_scalar(out=xn2[:, (j % 2) * 1024:(j % 2 + 1) * 1024],
                                in0=xs, scalar1=c_mean, scalar2=c_rstd,
                                op0=ALU.subtract, op1=ALU.mult).then_inc(S.semLN, 1)

            def emit_xnt(i):
                q.wait_ge(S.semTR, i + 1)
                if i >= 2:
                    q.wait_ge(S.semSPX, 16 * (i - 1))
                    q.wait_ge(S.semSC, 8 * (i - 1))
                q.tensor_copy(out=xnT2[:, (i % 2) * 1024:(i % 2 + 1) * 1024],
                              in_=p_tr[i % 2][:]).then_inc(S.semXNT, 1)

            for j in range(min(2, nt)):
                emit_ln(j)
            emit_xnt(0)
            for i in range(nt):
                if i + 1 < nt:
                    emit_xnt(i + 1)
                if i + 2 < nt:
                    emit_ln(i + 2)
                for qd in range(3):
                    q.wait_ge(S.semEV, 8 * i + 2 * qd + 2)
                    vs(q.tensor_tensor(out=sqb[:],
                                       in0=sc[:, qd * 2048:(qd + 1) * 2048],
                                       in1=sc[:, qd * 2048:(qd + 1) * 2048],
                                       op=ALU.mult))
                    vs(q.reduce_sum(out=ssq[:, qd:qd + 1], in_=sqb[:], axis=AX.X))
                q.wait_ge(S.semSQ, i + 1)
                q.reduce_sum(out=c_ss, in_=ssq[:, 0:5], axis=AX.X
                             ).then_inc(S.semSS, 1)
            for g in range(ng):
                for tb in (2, 3):
                    q.wait_ge(S.semBMM, 64 * (g + 1))
                    if g == 0 and tb == 2:
                        q.wait_ge(S.semSPG, 32 * nt)
                    if g == 1 and tb == 2:
                        q.wait_ge(S.semSPH, 32 * nt)
                    if g >= 2:
                        q.wait_ge(S.semSPS, 16 * (g - 1))
                    base = (g % 2) * 4096 + tb * 1024
                    q.tensor_copy(out=sB[:, base:base + 512],
                                  in_=s_ps[tb][:, 0:512])
                    q.tensor_copy(out=sB[:, base + 512:base + 1024],
                                  in_=s_ps[tb][:, 512:1024]).then_inc(S.semSEV, 1)
            for h in range(nh):
                for hb in range(24):
                    G2 = 24 * h + hb
                    q.wait_ge(S.semCEV, G2 + 1)
                    if hb == 0 and h == 0:
                        q.wait_ge(S.semSPS, 16 * ng)
                    if hb == 0 and h >= 1:
                        q.wait_ge(S.semCP2, 4 * h)
                    q.tensor_tensor(out=ch[:, hb * 256:(hb + 1) * 256],
                                    in0=sg[:, (hb % 2) * 256:(hb % 2 + 1) * 256],
                                    in1=pu_[hb % 2],
                                    op=ALU.mult).then_inc(S.semCH, 1)
                for tb in range(2):
                    q.wait_ge(S.semCGE, 4 * h + 2 * tb + 2)
                    q.wait_ge(S.semCL, 48 * (h + 1))
                    cxs = cxf[:, (h % 2) * 2048 + tb * 1024:
                              (h % 2) * 2048 + (tb + 1) * 1024]
                    vs(q.tensor_tensor(out=prodf[:, tb * 1024:(tb + 1) * 1024],
                                       in0=cs[:, (h % 2) * 2048 + tb * 1024:
                                              (h % 2) * 2048 + (tb + 1) * 1024],
                                       in1=cgf[:, tb * 1024:(tb + 1) * 1024],
                                       op=ALU.mult))
                    q.tensor_tensor(out=cxs, in0=cxs,
                                    in1=prodf[:, tb * 1024:(tb + 1) * 1024],
                                    op=ALU.add).then_inc(S.semCFIN, 1)

    return nc


def kernel(L0_attn_q, L0_attn_k, L0_attn_v, L0_ffn_w_gate, L0_ffn_w_up,
           L0_ffn_w_down):
    x = np.ascontiguousarray(np.asarray(L0_attn_q), dtype=np.float32)
    k = np.asarray(L0_attn_k)
    v = np.ascontiguousarray(np.asarray(L0_attn_v), dtype=np.float32)
    wg = np.asarray(L0_ffn_w_gate)
    wu = np.asarray(L0_ffn_w_up)
    wd = np.asarray(L0_ffn_w_down)

    kT = np.ascontiguousarray(k.T.astype(np.float32))
    wgT = np.ascontiguousarray(wg.T.astype(np.int32))
    wuT = np.ascontiguousarray(wu.T.astype(np.int32))
    wdT = np.ascontiguousarray(wd.T.astype(np.int32))
    ident = np.eye(128, dtype=ml_dtypes.bfloat16)

    nc = build(nt=64)
    in_maps = []
    for c in range(8):
        xc = np.ascontiguousarray(x[2 * c:2 * c + 2].reshape(8192, 1024))
        in_maps.append({"x": xc, "kT": kT, "v": v, "wgT": wgT, "wuT": wuT,
                        "wdT": wdT, "ident": ident})
    res = run_bass_kernel_spmd(nc, in_maps, core_ids=list(range(8)))
    out = np.concatenate(
        [np.asarray(r["out"], np.float32).reshape(2, 4096, 1024)
         for r in res.results], axis=0)
    return out


# version 22 (final)
# speedup vs baseline: 1.1930x; 1.1930x over previous
import numpy as np
import ml_dtypes
import concourse.bass as bass
import concourse.mybir as mybir
from concourse.bass_utils import run_bass_kernel_spmd

F32 = mybir.dt.float32
BF16 = mybir.dt.bfloat16
I32 = mybir.dt.int32
AF = mybir.ActivationFunctionType
ALU = mybir.AluOpType
AX = mybir.AxisListType

D = 1024
NB = 8192
HID = 3072
SQRT_N = float(np.sqrt(8192.0))
LN_SQRT_N = float(np.log(8192.0) / 2.0)
FFN_SCALE = 1.0 / 127.0
EPS = 1e-5
MAGIC = 0x5F3759DF


def _newton_rsqrt(vq, vs, dst, src, t1):
    vs(vq.tensor_scalar(out=t1.bitcast(I32), in0=src.bitcast(I32),
                        scalar1=1, scalar2=None, op0=ALU.arith_shift_right))
    vs(vq.tensor_scalar(out=dst.bitcast(I32), in0=t1.bitcast(I32),
                        scalar1=-1, scalar2=MAGIC, op0=ALU.mult, op1=ALU.add))
    for _ in range(2):
        vs(vq.tensor_tensor(out=t1, in0=dst, in1=dst, op=ALU.mult))
        vs(vq.tensor_tensor(out=t1, in0=t1, in1=src, op=ALU.mult))
        vs(vq.tensor_scalar(out=t1, in0=t1, scalar1=-0.5, scalar2=1.5,
                            op0=ALU.mult, op1=ALU.add))
        vs(vq.tensor_tensor(out=dst, in0=dst, in1=t1, op=ALU.mult))


def build(nt=64, dbg=False):
    T = nt * 128
    ng = nt // 4
    nh = nt // 2
    nc = bass.Bass("TRN2", target_bir_lowering=False)

    x_h = nc.dram_tensor("x", [T, D], F32, kind="ExternalInput").ap()
    kT_h = nc.dram_tensor("kT", [D, NB], F32, kind="ExternalInput").ap()
    v_h = nc.dram_tensor("v", [NB, D], F32, kind="ExternalInput").ap()
    wgT_h = nc.dram_tensor("wgT", [D, HID], I32, kind="ExternalInput").ap()
    wuT_h = nc.dram_tensor("wuT", [D, HID], I32, kind="ExternalInput").ap()
    wdT_h = nc.dram_tensor("wdT", [HID, D], I32, kind="ExternalInput").ap()
    id_h = nc.dram_tensor("ident", [128, 128], BF16, kind="ExternalInput").ap()
    out_h = nc.dram_tensor("out", [T, D], F32, kind="ExternalOutput").ap()

    scrk = "ExternalOutput" if dbg else "Internal"
    if dbg:
        dbg_xn = nc.dram_tensor("dbg_xn", [128, 2048], BF16, kind="ExternalOutput").ap()
        dbg_x = nc.dram_tensor("dbg_x", [128, 2048], F32, kind="ExternalOutput").ap()
        dbg_cols = nc.dram_tensor("dbg_cols", [128, 16], F32, kind="ExternalOutput").ap()
        dbg_ssq = nc.dram_tensor("dbg_ssq", [128, 8], F32, kind="ExternalOutput").ap()
        dbg_inv = nc.dram_tensor("dbg_inv", [128, 2], F32, kind="ExternalOutput").ap()
    g_h = nc.dram_tensor("g_scr", [T, NB], BF16, kind=scrk).ap()
    xnT_h = nc.dram_tensor("xnT_scr", [nt, 128, D], BF16, kind=scrk).ap()
    s_h = nc.dram_tensor("s_scr", [T, D], BF16, kind=scrk).ap()

    import contextlib
    ctx = contextlib.ExitStack()
    with ctx:
        arena = ctx.enter_context(nc.sbuf_tensor("arena", [128, 73728], BF16))
        work = ctx.enter_context(nc.sbuf_tensor("work", [128, 30720], BF16))
        ident = ctx.enter_context(nc.sbuf_tensor("ident_sb", [128, 128], BF16))
        
        cols = ctx.enter_context(nc.sbuf_tensor("cols", [128, 16], F32))
        ssq = ctx.enter_context(nc.sbuf_tensor("ssq", [128, 8], F32))
        inv2 = ctx.enter_context(nc.sbuf_tensor("inv2", [128, 2], F32))
        sg = ctx.enter_context(nc.sbuf_tensor("sg", [128, 512], BF16))
        su = ctx.enter_context(nc.sbuf_tensor("su", [128, 512], BF16))
        P = ctx.enter_context(nc.psum_tensor("ps", [128, 4096], F32))

        jk = work[:, 24576:25600]
        sqb = work[:, 28672:30720]
        x2f = work[:, 0:4096].bitcast(F32)
        xn2 = work[:, 4096:6144]
        xnT2 = work[:, 6144:8192]
        sc = work[:, 8192:16384]
        gt = work[:, 16384:24576]
        ring = work[:, 25600:28672]
        sB = work[:, 16384:24576]
        cxnT = work[:, 0:4096]
        cs = work[:, 4096:8192]
        cxf = work[:, 8192:16384].bitcast(F32)
        ch = work[:, 16384:22528]
        cgf = work[:, 22528:26624].bitcast(F32)
        prodf = work[:, 26624:30720].bitcast(F32)

        c_sum = cols[:, 0:1]
        c_mean = cols[:, 1:2]
        sqs2 = cols[:, 2:4]
        c_var = cols[:, 4:5]
        c_rstd = cols[:, 5:6]
        c_t1 = cols[:, 6:7]
        c_ss = cols[:, 8:9]
        c_y = cols[:, 9:10]
        c_jk = cols[:, 10:11]

        p_sc = [P[:, 0:1024], P[:, 1024:2048], P[:, 2048:3072]]
        p_tr = [P[:, 3072:3584].bitcast(BF16), P[:, 3584:4096].bitcast(BF16)]
        s_ps = [P[:, tb * 1024:(tb + 1) * 1024] for tb in range(4)]
        pg_ = [P[:, 0:256], P[:, 512:768]]
        pu_ = [P[:, 1024:1280], P[:, 1536:1792]]
        pd_ = [P[:, 2048 + i * 512:2048 + (i + 1) * 512] for i in range(4)]

        sems = {}
        for name in ["semX", "semW", "semSPX", "semSPG", "semSPS",
                     "semT0", "semT1", "semT2", "semT3", "semT4", "semT5",
                     "semCL", "semOUT", "semLN", "semLNS", "semTR", "semXNT",
                     "semSC", "semEV", "semINV", "semGEL", "semBMM", "semSEV",
                     "semCP1", "semCEV", "semCH", "semCP2", "semCGE", "semCFIN",
                     "semV", "semSQ", "semGL", "semSPH", "semSS", "semSR",
                     "semBL"]:
            sems[name] = ctx.enter_context(nc.semaphore(name))
        S = type("S", (), sems)
        semT = [sems["semT%d" % s] for s in range(6)]

        blk = ctx.enter_context(nc.Block())

        @blk.sync
        def _(q):
            q.dma_start(out=ident[:], in_=id_h).then_inc(S.semX, 16)
            q.wait_ge(S.semX, 16)
            for j in range(min(2, nt)):
                q.dma_start(out=x2f[:, (j % 2) * 1024:(j % 2 + 1) * 1024],
                            in_=x_h[j * 128:(j + 1) * 128, :]).then_inc(S.semX, 16)
                q.wait_ge(S.semX, 16 * (j + 2))
            for i in range(nt):
                if 8 <= i < 14:
                    k = i - 8
                    if k == 0:
                        q.wait_ge(S.semSPG, 128)
                    q.dma_start_transpose(
                        ring[:, k * 512:(k + 1) * 512],
                        g_h[0:512, k * 128:(k + 1) * 128],
                    ).then_inc(semT[k], 16)
                q.wait_ge(S.semXNT, i + 1)
                q.dma_start(out=xnT_h[i], in_=xnT2[:, (i % 2) * 1024:(i % 2 + 1) * 1024]
                            ).then_inc(S.semSPX, 16)
                q.wait_ge(S.semSPX, 16 * (i + 1))
                j = i + 2
                if j < nt:
                    q.wait_ge(S.semLN, i + 1)
                    q.wait_ge(S.semLNS, i + 1)
                    q.dma_start(out=x2f[:, (j % 2) * 1024:(j % 2 + 1) * 1024],
                                in_=x_h[j * 128:(j + 1) * 128, :]).then_inc(S.semX, 16)
                    q.wait_ge(S.semX, 16 * (j + 2))
            def ld_cxnT(h):
                q.dma_start(
                    out=cxnT[:, (h % 2) * 2048:(h % 2 + 1) * 2048].rearrange(
                        "p (c T t) -> p c T t", c=8, T=2),
                    in_=xnT_h[2 * h:2 * h + 2].rearrange(
                        "T p (c t) -> p c T t", c=8),
                ).then_inc(S.semCL, 16)

            def ld_cs(h):
                q.dma_start(
                    out=cs[:, (h % 2) * 2048:(h % 2 + 1) * 2048].rearrange(
                        "p (tb o) -> p tb o", tb=2),
                    in_=s_h[256 * h:256 * (h + 1), :].rearrange(
                        "(tb p) o -> p tb o", p=128),
                ).then_inc(S.semCL, 16)

            def ld_cxf(h):
                q.dma_start(
                    out=cxf[:, (h % 2) * 2048:(h % 2 + 1) * 2048].rearrange(
                        "p (tb o) -> p tb o", tb=2),
                    in_=x_h[256 * h:256 * (h + 1), :].rearrange(
                        "(tb p) o -> p tb o", p=128),
                ).then_inc(S.semCL, 16)

            for g in range(ng):
                if g == 1:
                    q.wait_ge(S.semSPX, 16 * nt)
                    q.wait_ge(S.semLN, nt)
                    q.wait_ge(S.semLNS, nt)
                    ld_cxnT(0)
                if g == 2:
                    q.wait_ge(S.semTR, nt)
                    q.wait_ge(S.semSC, 8 * nt)
                    q.wait_ge(S.semSPS, 16)
                    q.wait_ge(S.semCL, 16)
                    ld_cs(0)
                if g == 3:
                    q.wait_ge(S.semGEL, nt)
                    q.wait_ge(S.semCL, 32)
                    ld_cxf(0)
                if g == 4:
                    q.wait_ge(S.semCL, 48)
                    ld_cxnT(1)
                if g == 5:
                    q.wait_ge(S.semCL, 64)
                    ld_cs(1)
                if g == 6:
                    q.wait_ge(S.semCL, 80)
                    ld_cxf(1)
                for c in range(64):
                    if g == 0 and c < 6:
                        continue
                    G = 64 * g + c
                    if c == 0:
                        q.wait_ge(S.semSPG, 32 * (4 * g + 4))
                    if c == 32:
                        q.wait_ge(S.semSPH, 32 * (4 * g + 4))
                    if G >= 6:
                        q.wait_ge(S.semBMM, G - 5)
                    q.dma_start_transpose(
                        ring[:, (G % 6) * 512:(G % 6 + 1) * 512],
                        g_h[512 * g:512 * (g + 1), c * 128:(c + 1) * 128],
                    ).then_inc(semT[G % 6], 16)
            for h in range(2, nh):
                q.wait_ge(S.semOUT, 32 * (h - 1))
                q.wait_ge(S.semSPS, 16 * ng)
                ld_cxnT(h)
                q.wait_ge(S.semCL, 48 * h + 16)
                ld_cs(h)
                q.wait_ge(S.semCL, 48 * h + 32)
                ld_cxf(h)
            q.wait_ge(S.semOUT, 32 * nh)
            if dbg:
                q.dma_start(out=dbg_xn, in_=xn2).then_inc(S.semX, 16)
                q.dma_start(out=dbg_x, in_=x2f).then_inc(S.semX, 16)
                q.dma_start(out=dbg_cols, in_=cols[:]).then_inc(S.semX, 16)
                q.dma_start(out=dbg_ssq, in_=ssq[:]).then_inc(S.semX, 16)
                q.dma_start(out=dbg_inv, in_=inv2[:]).then_inc(S.semX, 16)
                q.wait_ge(S.semX, 16 * (nt + 1) + 80)

        @blk.gpsimd
        def _(q):
            kT_arena = arena[:, 0:65536].rearrange("p (d n) -> p d n", d=8)
            for j in range(8):
                q.dma_start(out=kT_arena[:, :, j * 1024:(j + 1) * 1024],
                            in_=kT_h[:, j * 1024:(j + 1) * 1024].rearrange(
                                "(d p) n -> p d n", p=128)).then_inc(S.semW, 16)
                q.wait_ge(S.semW, 16 * (j + 1))
            for i in range(nt - 1):
                q.wait_ge(S.semGL, i + 1)
                q.dma_start(out=g_h[128 * i:128 * (i + 1), 0:2048],
                            in_=gt[:, 0:2048]).then_inc(S.semSPG, 16)
                q.dma_start(out=g_h[128 * i:128 * (i + 1), 2048:4096],
                            in_=gt[:, 2048:4096]).then_inc(S.semSPG, 16)
            q.wait_ge(S.semSC, 8 * nt)
            for m in range(16):
                q.dma_start(
                    out=arena[:, m * 4096:(m + 1) * 4096].rearrange(
                        "p (c o) -> p c o", c=4),
                    in_=v_h[512 * m:512 * (m + 1), :].rearrange(
                        "(c p) o -> p c o", p=128),
                ).then_inc(S.semW, 16)
                q.wait_ge(S.semW, 128 + 16 * (m + 1))
            q.wait_ge(S.semGL, nt)
            q.dma_start(out=g_h[128 * (nt - 1):128 * nt, 0:2048],
                        in_=gt[:, 0:2048]).then_inc(S.semSPG, 16)
            q.dma_start(out=g_h[128 * (nt - 1):128 * nt, 2048:4096],
                        in_=gt[:, 2048:4096]).then_inc(S.semSPG, 16)
            wg_arena = arena[:, 0:24576].rearrange("p (d hh) -> p d hh", d=8)
            wu_arena = arena[:, 24576:49152].rearrange("p (d hh) -> p d hh", d=8)
            q.wait_ge(S.semBMM, 64 * (ng - 1) + 24)
            for j in range(4):
                q.dma_start(
                    out=wg_arena[:, :, 768 * j:768 * (j + 1)],
                    in_=wgT_h[:, 768 * j:768 * (j + 1)].rearrange(
                        "(d p) hh -> p d hh", p=128),
                ).then_inc(S.semW, 16)
                q.wait_ge(S.semW, 384 + 16 * (j + 1))
            q.wait_ge(S.semBMM, 64 * (ng - 1) + 48)
            for j in range(4):
                q.dma_start(
                    out=wu_arena[:, :, 768 * j:768 * (j + 1)],
                    in_=wuT_h[:, 768 * j:768 * (j + 1)].rearrange(
                        "(d p) hh -> p d hh", p=128),
                ).then_inc(S.semW, 16)
                q.wait_ge(S.semW, 448 + 16 * (j + 1))
            q.wait_ge(S.semBMM, 64 * ng)
            for j in range(4):
                q.dma_start(
                    out=arena[:, 49152 + 6 * j * 1024:49152 + 6 * (j + 1) * 1024
                              ].rearrange("p (c o) -> p c o", c=6),
                    in_=wdT_h[768 * j:768 * (j + 1), :].rearrange(
                        "(c p) o -> p c o", p=128),
                ).then_inc(S.semW, 16)
                q.wait_ge(S.semW, 512 + 16 * (j + 1))
            for h in range(nh):
                for tb in range(2):
                    q.wait_ge(S.semCFIN, 2 * h + tb + 1)
                    q.dma_start(
                        out=out_h[256 * h + 128 * tb:256 * h + 128 * (tb + 1), :],
                        in_=cxf[:, (h % 2) * 2048 + tb * 1024:
                                (h % 2) * 2048 + (tb + 1) * 1024],
                    ).then_inc(S.semOUT, 16)
            q.wait_ge(S.semOUT, 32 * nh)

        @blk.tensor
        def _(q):
            def emit_tr(j):
                q.wait_ge(S.semLN, j + 1)
                if j >= 2:
                    q.wait_ge(S.semXNT, j - 1)
                inst = None
                for c in range(8):
                    inst = q.transpose(
                        p_tr[j % 2][:, c * 128:(c + 1) * 128],
                        xn2[:, (j % 2) * 1024 + c * 128:(j % 2) * 1024 + (c + 1) * 128],
                        ident[:])
                inst.then_inc(S.semTR, 1)

            for i in range(nt):
                if i == 0:
                    emit_tr(0)
                if i + 1 < nt:
                    emit_tr(i + 1)
                q.wait_ge(S.semXNT, i + 1)
                for qq in range(8):
                    g8 = 8 * i + qq
                    if i < 4:
                        q.wait_ge(S.semW, 16 * (qq + 1))
                    if qq >= 3:
                        q.wait_ge(S.semEV, 8 * i + qq - 2)
                    elif i >= 1:
                        q.wait_ge(S.semEV, 8 * (i - 1) + {0: 6, 1: 7, 2: 5}[qq] + 1)
                    ps = p_sc[qq % 3]
                    inst = None
                    for d in range(8):
                        stat = xnT2[:, (i % 2) * 1024 + d * 128:
                                    (i % 2) * 1024 + (d + 1) * 128]
                        for nb in range(2):
                            n0 = qq * 1024 + nb * 512
                            inst = q.matmul(
                                ps[:, nb * 512:(nb + 1) * 512], stat,
                                arena[:, d * 8192 + n0:d * 8192 + n0 + 512],
                                start=(d == 0), stop=(d == 7))
                    inst.then_inc(S.semSC, 1)
            for g in range(ng):
                for c in range(64):
                    G = 64 * g + c
                    q.wait_ge(semT[G % 6], 16 * (G // 6 + 1))
                    if c % 4 == 0:
                        m = c // 4
                        q.wait_ge(S.semW, 128 + 16 * (m + 1))
                    if c == 0:
                        if g == 0:
                            q.wait_ge(S.semEV, 8 * nt)
                        if g >= 1:
                            q.wait_ge(S.semSEV, 4 * g)
                    slot = ring[:, (G % 6) * 512:(G % 6 + 1) * 512]
                    inst = None
                    for tb in range(4):
                        stat = slot[:, tb * 128:(tb + 1) * 128]
                        for oh in range(2):
                            inst = q.matmul(
                                s_ps[tb][:, oh * 512:(oh + 1) * 512], stat,
                                arena[:, c * 1024 + oh * 512:c * 1024 + oh * 512 + 512],
                                start=(c == 0), stop=(c == 63))
                        if c == 63 and tb < 3:
                            inst.then_inc(S.semBL, 1)
                    inst.then_inc(S.semBMM, 1)
            for h in range(nh):
                for hb in range(24):
                    G2 = 24 * h + hb
                    if hb == 0:
                        q.wait_ge(S.semCL, 48 * h + 16)
                        if h == 0:
                            q.wait_ge(S.semSEV, 4 * ng)
                    if h == 0:
                        q.wait_ge(S.semW, 384 + 16 * (hb // 6 + 1))
                    if G2 >= 2:
                        q.wait_ge(S.semCEV, G2 - 1)
                        q.wait_ge(S.semCH, G2 - 1)
                    pg = pg_[hb % 2]
                    pu = pu_[hb % 2]
                    inst = None
                    for d in range(8):
                        mv = cxnT[:, (h % 2) * 2048 + d * 256:
                                  (h % 2) * 2048 + (d + 1) * 256]
                        q.matmul(pg, arena[:, d * 3072 + hb * 128:
                                           d * 3072 + (hb + 1) * 128],
                                 mv, start=(d == 0), stop=(d == 7))
                    if h == 0:
                        q.wait_ge(S.semW,
                                  448 + 16 * (hb // 6 + 1))
                    for d in range(8):
                        mv = cxnT[:, (h % 2) * 2048 + d * 256:
                                  (h % 2) * 2048 + (d + 1) * 256]
                        inst = q.matmul(pu, arena[:, 24576 + d * 3072 + hb * 128:
                                                  24576 + d * 3072 + (hb + 1) * 128],
                                        mv, start=(d == 0), stop=(d == 7))
                    inst.then_inc(S.semCP1, 1)
                for idx in range(4):
                    tb, oh = divmod(idx, 2)
                    if h >= 1:
                        q.wait_ge(S.semCGE, 4 * (h - 1) + idx + 1)
                    inst = None
                    for hb in range(24):
                        if idx == 0:
                            q.wait_ge(S.semCH, 24 * h + hb + 1)
                        if h == 0 and hb % 6 == 0:
                            q.wait_ge(S.semW, 512 + 16 * (hb // 6 + 1))
                        inst = q.matmul(
                            pd_[idx],
                            ch[:, hb * 256 + tb * 128:hb * 256 + (tb + 1) * 128],
                            arena[:, 49152 + hb * 1024 + oh * 512:
                                  49152 + hb * 1024 + oh * 512 + 512],
                            start=(hb == 0), stop=(hb == 23))
                    inst.then_inc(S.semCP2, 1)

        @blk.scalar
        def _(q):
            def emit_sq(j):
                q.wait_ge(S.semX, 16 * (j + 2))
                if j >= 2:
                    q.wait_ge(S.semLN, j - 1)
                q.activation(out=jk[:], in_=x2f[:, (j % 2) * 1024:(j % 2 + 1) * 1024],
                             func=AF.Square,
                             accum_out=sqs2[:, (j % 2):(j % 2) + 1]
                             ).then_inc(S.semLNS, 1)

            def emit_ge8(i, e):
                q.wait_ge(S.semEV, 8 * i + e + 1)
                if e == 0:
                    q.wait_ge(S.semINV, i + 1)
                    if i >= 1:
                        q.wait_ge(S.semSPG, 32 * i)
                if e == 4 and i >= 1:
                    q.wait_ge(S.semSPH, 32 * i)
                inst = q.activation(
                    out=gt[:, e * 1024:(e + 1) * 1024],
                    in_=sc[:, e * 1024:(e + 1) * 1024], func=AF.Gelu,
                    scale=inv2[:, (i % 2):(i % 2) + 1])
                if e == 3:
                    inst.then_inc(S.semGL, 1)
                if e == 7:
                    inst.then_inc(S.semGEL, 1)
                    q.wait_ge(S.semGEL, i + 1)
                    q.dma_start(out=g_h[128 * i:128 * (i + 1), 4096:6144],
                                in_=gt[:, 4096:6144]).then_inc(S.semSPH, 16)
                    q.dma_start(out=g_h[128 * i:128 * (i + 1), 6144:8192],
                                in_=gt[:, 6144:8192]).then_inc(S.semSPH, 16)

            def emit_ev(i, qq):
                q.wait_ge(S.semSC, 8 * i + qq + 1)
                q.activation(out=sc[:, qq * 1024:(qq + 1) * 1024],
                             in_=p_sc[qq % 3], func=AF.Copy
                             ).then_inc(S.semEV, 1)

            for j in range(min(2, nt)):
                emit_sq(j)
            for i in range(nt):
                for e in range(4):
                    if i >= 1:
                        emit_ge8(i - 1, e)
                    emit_ev(i, e)
                if i >= 1:
                    emit_ge8(i - 1, 4)
                if i + 2 < nt:
                    emit_sq(i + 2)
                if i >= 1:
                    emit_ge8(i - 1, 5)
                emit_ev(i, 4)
                if i >= 1:
                    emit_ge8(i - 1, 6)
                emit_ev(i, 5)
                if i >= 1:
                    emit_ge8(i - 1, 7)
                    q.activation(out=c_jk, in_=sqs2[:, 0:1], func=AF.Ln)
                emit_ev(i, 6)
                q.wait_ge(S.semEV, 8 * i + 7)
                q.activation(out=jk[:], in_=sc[:, 6144:7168], func=AF.Square,
                             accum_out=ssq[:, 3:4])
                emit_ev(i, 7)
                q.wait_ge(S.semEV, 8 * (i + 1))
                q.activation(out=jk[:], in_=sc[:, 7168:8192], func=AF.Square,
                             accum_out=ssq[:, 4:5]).then_inc(S.semSQ, 1)
                q.wait_ge(S.semSS, i + 1)
                q.activation(out=c_y, in_=c_ss, func=AF.Ln,
                             scale=1.0 / NB).then_inc(S.semSR, 1)
                q.wait_ge(S.semSR, i + 1)
                q.activation(out=inv2[:, (i % 2):(i % 2) + 1], in_=c_y,
                             func=AF.Exp, scale=-0.5).then_inc(S.semINV, 1)
            for e in range(8):
                emit_ge8(nt - 1, e)
            for g in range(ng):
                for tb in range(2):
                    q.wait_ge(S.semBL, 3 * g + tb + 1)
                    if g == 0 and tb == 0:
                        q.wait_ge(S.semSPG, 32 * nt)
                    if g == 1 and tb == 0:
                        q.wait_ge(S.semSPH, 32 * nt)
                    if g >= 2:
                        q.wait_ge(S.semSPS, 16 * (g - 1))
                    q.activation(out=sB[:, (g % 2) * 4096 + tb * 1024:
                                       (g % 2) * 4096 + (tb + 1) * 1024],
                                 in_=s_ps[tb], func=AF.Copy).then_inc(S.semSEV, 1)
                q.wait_ge(S.semSEV, 4 * (g + 1))
                q.dma_start(
                    out=s_h[512 * g:512 * (g + 1), :].rearrange(
                        "(tb p) o -> p tb o", p=128),
                    in_=sB[:, (g % 2) * 4096:(g % 2 + 1) * 4096].rearrange(
                        "p (tb o) -> p tb o", tb=4),
                ).then_inc(S.semSPS, 16)
            for h in range(nh):
                for hb in range(24):
                    G2 = 24 * h + hb
                    q.wait_ge(S.semCP1, G2 + 1)
                    if G2 >= 2:
                        q.wait_ge(S.semCH, G2 - 1)
                    q.activation(out=sg[:, (hb % 2) * 256:(hb % 2 + 1) * 256],
                                 in_=pg_[hb % 2], func=AF.Silu, scale=FFN_SCALE
                                 ).then_inc(S.semCEV, 1)
                for idx in range(4):
                    tb, oh = divmod(idx, 2)
                    q.wait_ge(S.semCP2, 4 * h + idx + 1)
                    if idx == 0 and h == 0:
                        q.wait_ge(S.semSPS, 16 * ng)
                    if idx == 0 and h >= 1:
                        q.wait_ge(S.semCFIN, 2 * h)
                    q.activation(out=cgf[:, tb * 1024 + oh * 512:
                                        tb * 1024 + oh * 512 + 512],
                                 in_=pd_[idx], func=AF.Silu,
                                 scale=FFN_SCALE * FFN_SCALE
                                 ).then_inc(S.semCGE, 1)

        @blk.vector
        def _(q):
            vcnt = [0]

            def vs(inst):
                vcnt[0] += 1
                inst.then_inc(S.semV, 1)
                q.wait_ge(S.semV, vcnt[0])

            def emit_ln(j):
                xs = x2f[:, (j % 2) * 1024:(j % 2 + 1) * 1024]
                q.wait_ge(S.semX, 16 * (j + 2))
                vs(q.reduce_sum(out=c_sum, in_=xs, axis=AX.X))
                vs(q.tensor_scalar(out=c_mean, in0=c_sum, scalar1=1.0 / D,
                                   scalar2=None, op0=ALU.mult))
                q.wait_ge(S.semLNS, j + 1)
                vs(q.tensor_scalar(out=c_var, in0=sqs2[:, (j % 2):(j % 2) + 1],
                                   scalar1=1.0 / D, scalar2=None, op0=ALU.mult))
                vs(q.tensor_tensor(out=c_t1, in0=c_mean, in1=c_mean, op=ALU.mult))
                vs(q.tensor_tensor(out=c_var, in0=c_var, in1=c_t1, op=ALU.subtract))
                vs(q.tensor_scalar(out=c_var, in0=c_var, scalar1=EPS, scalar2=None,
                                   op0=ALU.add))
                _newton_rsqrt(q, vs, c_rstd, c_var, c_t1)
                if j >= 2:
                    q.wait_ge(S.semTR, j - 1)
                q.tensor_scalar(out=xn2[:, (j % 2) * 1024:(j % 2 + 1) * 1024],
                                in0=xs, scalar1=c_mean, scalar2=c_rstd,
                                op0=ALU.subtract, op1=ALU.mult).then_inc(S.semLN, 1)

            def emit_xnt(i):
                q.wait_ge(S.semTR, i + 1)
                if i >= 2:
                    q.wait_ge(S.semSPX, 16 * (i - 1))
                    q.wait_ge(S.semSC, 8 * (i - 1))
                q.tensor_copy(out=xnT2[:, (i % 2) * 1024:(i % 2 + 1) * 1024],
                              in_=p_tr[i % 2][:]).then_inc(S.semXNT, 1)

            for j in range(min(2, nt)):
                emit_ln(j)
            emit_xnt(0)
            for i in range(nt):
                if i + 1 < nt:
                    emit_xnt(i + 1)
                if i + 2 < nt:
                    emit_ln(i + 2)
                for qd in range(3):
                    q.wait_ge(S.semEV, 8 * i + 2 * qd + 2)
                    vs(q.tensor_tensor(out=sqb[:],
                                       in0=sc[:, qd * 2048:(qd + 1) * 2048],
                                       in1=sc[:, qd * 2048:(qd + 1) * 2048],
                                       op=ALU.mult))
                    vs(q.reduce_sum(out=ssq[:, qd:qd + 1], in_=sqb[:], axis=AX.X))
                q.wait_ge(S.semSQ, i + 1)
                q.reduce_sum(out=c_ss, in_=ssq[:, 0:5], axis=AX.X
                             ).then_inc(S.semSS, 1)
            for g in range(ng):
                for tb in (2, 3):
                    if tb == 2:
                        q.wait_ge(S.semBL, 3 * g + 3)
                    else:
                        q.wait_ge(S.semBMM, 64 * (g + 1))
                    if g == 0 and tb == 2:
                        q.wait_ge(S.semSPG, 32 * nt)
                    if g == 1 and tb == 2:
                        q.wait_ge(S.semSPH, 32 * nt)
                    if g >= 2:
                        q.wait_ge(S.semSPS, 16 * (g - 1))
                    base = (g % 2) * 4096 + tb * 1024
                    q.tensor_copy(out=sB[:, base:base + 512],
                                  in_=s_ps[tb][:, 0:512])
                    q.tensor_copy(out=sB[:, base + 512:base + 1024],
                                  in_=s_ps[tb][:, 512:1024]).then_inc(S.semSEV, 1)
            for h in range(nh):
                for hb in range(24):
                    G2 = 24 * h + hb
                    q.wait_ge(S.semCEV, G2 + 1)
                    if hb == 0 and h == 0:
                        q.wait_ge(S.semSPS, 16 * ng)
                    if hb == 0 and h >= 1:
                        q.wait_ge(S.semCP2, 4 * h)
                    q.tensor_tensor(out=ch[:, hb * 256:(hb + 1) * 256],
                                    in0=sg[:, (hb % 2) * 256:(hb % 2 + 1) * 256],
                                    in1=pu_[hb % 2],
                                    op=ALU.mult).then_inc(S.semCH, 1)
                for tb in range(2):
                    q.wait_ge(S.semCGE, 4 * h + 2 * tb + 2)
                    q.wait_ge(S.semCL, 48 * (h + 1))
                    cxs = cxf[:, (h % 2) * 2048 + tb * 1024:
                              (h % 2) * 2048 + (tb + 1) * 1024]
                    vs(q.tensor_tensor(out=prodf[:, tb * 1024:(tb + 1) * 1024],
                                       in0=cs[:, (h % 2) * 2048 + tb * 1024:
                                              (h % 2) * 2048 + (tb + 1) * 1024],
                                       in1=cgf[:, tb * 1024:(tb + 1) * 1024],
                                       op=ALU.mult))
                    q.tensor_tensor(out=cxs, in0=cxs,
                                    in1=prodf[:, tb * 1024:(tb + 1) * 1024],
                                    op=ALU.add).then_inc(S.semCFIN, 1)

    return nc


def kernel(L0_attn_q, L0_attn_k, L0_attn_v, L0_ffn_w_gate, L0_ffn_w_up,
           L0_ffn_w_down):
    x = np.ascontiguousarray(np.asarray(L0_attn_q), dtype=np.float32)
    k = np.asarray(L0_attn_k)
    v = np.ascontiguousarray(np.asarray(L0_attn_v), dtype=np.float32)
    wg = np.asarray(L0_ffn_w_gate)
    wu = np.asarray(L0_ffn_w_up)
    wd = np.asarray(L0_ffn_w_down)

    kT = np.ascontiguousarray(k.T.astype(np.float32))
    wgT = np.ascontiguousarray(wg.T.astype(np.int32))
    wuT = np.ascontiguousarray(wu.T.astype(np.int32))
    wdT = np.ascontiguousarray(wd.T.astype(np.int32))
    ident = np.eye(128, dtype=ml_dtypes.bfloat16)

    nc = build(nt=64)
    in_maps = []
    for c in range(8):
        xc = np.ascontiguousarray(x[2 * c:2 * c + 2].reshape(8192, 1024))
        in_maps.append({"x": xc, "kT": kT, "v": v, "wgT": wgT, "wuT": wuT,
                        "wdT": wdT, "ident": ident})
    res = run_bass_kernel_spmd(nc, in_maps, core_ids=list(range(8)))
    out = np.concatenate(
        [np.asarray(r["out"], np.float32).reshape(2, 4096, 1024)
         for r in res.results], axis=0)
    return out
